# version 20 (final)
# baseline (speedup 1.0000x reference)
import numpy as np
import ml_dtypes
import concourse.bass as bass
import concourse.mybir as mybir
from concourse.bass_utils import run_bass_kernel_spmd

F32 = mybir.dt.float32
F16 = mybir.dt.float16
BF16 = mybir.dt.bfloat16
FP8 = mybir.dt.float8e4
AF = mybir.ActivationFunctionType
ALU = mybir.AluOpType

D = 1024
S = 4096
T = 512
NT = S // T
DC = D // 128
HC = 4096 // 128
NCORES = 8


def _threefry2x32(k0, k1, x0, x1):
    rotations = [[13, 15, 26, 6], [17, 29, 16, 24]]
    k0 = np.uint32(k0)
    k1 = np.uint32(k1)
    ks2 = np.uint32(np.uint32(0x1BD11BDA) ^ k0 ^ k1)
    x0 = (x0 + k0).astype(np.uint32)
    x1 = (x1 + k1).astype(np.uint32)

    def rotl(x, r):
        return ((x << np.uint32(r)) | (x >> np.uint32(32 - r))).astype(np.uint32)

    ks = [k0, k1, ks2]
    for i in range(5):
        for r in rotations[i % 2]:
            x0 = (x0 + x1).astype(np.uint32)
            x1 = rotl(x1, r)
            x1 = (x1 ^ x0).astype(np.uint32)
        x0 = (x0 + ks[(i + 1) % 3]).astype(np.uint32)
        x1 = (x1 + ks[(i + 2) % 3] + np.uint32(i + 1)).astype(np.uint32)
    return x0, x1


def _dropout_scale_mask():
    n = NCORES * S * D
    idx = np.arange(n, dtype=np.uint64)
    c1 = (idx >> np.uint64(32)).astype(np.uint32)
    c2 = (idx & np.uint64(0xFFFFFFFF)).astype(np.uint32)
    r0, r1 = _threefry2x32(0, 42, c1, c2)
    bits = (r0 ^ r1).astype(np.uint32)
    fl = ((bits >> np.uint32(9)) | np.uint32(0x3F800000)).view(np.float32) - np.float32(1.0)
    keep = np.maximum(np.float32(0.0), fl) < np.float32(0.8)
    m = np.where(keep, np.float32(1.25), np.float32(0.0)).astype(ml_dtypes.float8_e4m3)
    return m.reshape(NCORES, S, D)


def _build_nc():
    nc = bass.Bass("TRN2", target_bir_lowering=False, detect_race_conditions=False)

    xT = nc.dram_tensor("xT", [D, S], F32, kind="ExternalInput")
    maskT = nc.dram_tensor("maskT", [D, S], FP8, kind="ExternalInput")
    W_in = nc.dram_tensor("W_in", [D, 2048], F32, kind="ExternalInput")
    W_out = nc.dram_tensor("W_out", [D, D], F32, kind="ExternalInput")
    w1T = nc.dram_tensor("w1T", [D, 4096], F32, kind="ExternalInput")
    w2T = nc.dram_tensor("w2T", [4096, D], F32, kind="ExternalInput")
    b_in_c = nc.dram_tensor("b_in_c", [128, 16], F32, kind="ExternalInput")
    b_out_c = nc.dram_tensor("b_out_c", [128, 8], F32, kind="ExternalInput")
    b1_c = nc.dram_tensor("b1_c", [128, 32], F32, kind="ExternalInput")
    b2_c = nc.dram_tensor("b2_c", [128, 8], F32, kind="ExternalInput")
    yout = nc.dram_tensor("yout", [D, S], F32, kind="ExternalOutput")

    from contextlib import ExitStack
    es = ExitStack()
    sb = lambda nm, shp, dt: es.enter_context(nc.sbuf_tensor(nm, shp, dt))
    sem = lambda nm: es.enter_context(nc.semaphore(nm))
    with es:
        wInB = sb("wInB", [128, DC * 2048], BF16)
        wOutB = sb("wOutB", [128, DC * 1024], BF16)
        w1Q = sb("w1Q", [128, DC * 4096], FP8)
        w2Q = sb("w2Q", [128, HC * 1024], FP8)
        xbuf = [sb("xbuf0", [128, DC * T], F32), sb("xbuf1", [128, DC * T], F32)]
        maskb = sb("maskb", [128, 2 * T], FP8)
        xnB = sb("xnB", [128, DC * T], BF16)
        xqQ = sb("xqQ", [128, DC * T], FP8)
        hB = sb("hB", [128, DC * T], BF16)
        hqQ = sb("hqQ", [128, HC * T], FP8)
        y16 = sb("y16", [128, 2 * T], F16)
        tb = sb("tb", [128, T], F32)
        sqB = sb("sqB", [128, DC * T], F16)
        phbB = sb("phbB", [128, 2 * T], BF16)
        sg = sb("sg", [128, 2 * T], BF16)
        sh = sb("sh", [128, 2 * T], BF16)
        h16 = sb("h16", [128, 2 * T], F16)
        outst = sb("outst", [128, 2 * T], F32)
        dec = sb("dec", [128, T], F32)
        epsC = sb("epsC", [128, 1], F32)
        onesB = sb("onesB", [128, 128], BF16)
        binC = sb("binC", [128, 16], F32)
        boutC = sb("boutC", [128, 8], F32)
        b1C = sb("b1C", [128, 32], F32)
        b1q8 = sb("b1q8", [128, 32], FP8)
        b1qF = sb("b1qF", [128, 32], F32)
        b2C = sb("b2C", [128, 8], F32)
        b2q8 = sb("b2q8", [128, 8], FP8)
        b2qF = sb("b2qF", [128, 8], F32)
        ps_y = [es.enter_context(nc.psum_tensor(f"ps_y{i}", [128, T], F32)) for i in range(2)]
        ps_pg = es.enter_context(nc.psum_tensor("ps_pg", [128, T], F32))
        ps_ph = es.enter_context(nc.psum_tensor("ps_ph", [128, T], F32))
        ps_h1 = [es.enter_context(nc.psum_tensor(f"ps_h1{i}", [128, T], F32)) for i in range(2)]
        ps_a = [es.enter_context(nc.psum_tensor(f"ps_a{i}", [128, T], F32)) for i in range(2)]

        sm_b = sem("sm_b")
        sm_xa = sem("sm_xa")
        sm_xb = sem("sm_xb")
        sm_xe = sem("sm_xe")
        sm_xf = sem("sm_xf")
        sm_xc = sem("sm_xc")
        sm_xd = sem("sm_xd")
        sm_m0 = sem("sm_m0")
        sm_m1 = sem("sm_m1")
        sm_wi = sem("sm_wi")
        sm_w1a = sem("sm_w1a")
        sm_w1b = sem("sm_w1b")
        sm_wo = sem("sm_wo")
        sm_w2 = sem("sm_w2")
        sm_st0 = sem("sm_st0")
        sm_st1 = sem("sm_st1")
        sm_bq = sem("sm_bq")
        sm_xbf = sem("sm_xbf")
        sm_smm = sem("sm_smm")
        sm_v0 = sem("sm_v0")
        sm_ln = sem("sm_ln")
        sm_xn = sem("sm_xn")
        sm_P = sem("sm_P")
        sm_gev = sem("sm_gev")
        sm_u = sem("sm_u")
        sm_A = sem("sm_A")
        sm_t1 = sem("sm_t1")
        sm_W1 = sem("sm_W1")
        sm_h16 = sem("sm_h16")
        sm_hq = sem("sm_hq")
        sm_W2 = sem("sm_W2")
        sm_y16 = sem("sm_y16")
        sm_fin = sem("sm_fin")

        block = es.enter_context(nc.Block())

        xT3 = xT.ap().rearrange("(c p) n -> p c n", p=128)
        maskT3 = maskT.ap().rearrange("(c p) n -> p c n", p=128)

        @block.sync
        def _(sync):
            x03 = xbuf[0].ap().rearrange("p (c n) -> p c n", n=T)
            sync.dma_start(x03[:, 0:2, :], xT3[:, 0:2, 0:T]).then_inc(sm_xa, 16)
            sync.dma_start(x03[:, 4:6, :], xT3[:, 4:6, 0:T]).then_inc(sm_xe, 16)
            sync.dma_start(binC[:], b_in_c.ap()).then_inc(sm_b, 16)
            sync.dma_start(boutC[:], b_out_c.ap()).then_inc(sm_b, 16)
            sync.dma_start(b1C[:], b1_c.ap()).then_inc(sm_b, 16)
            sync.dma_start(b2C[:], b2_c.ap()).then_inc(sm_b, 16)
            for k in range(1, NT):
                if k >= 2:
                    sync.wait_ge(sm_fin, 8 * (k - 2) + 8)
                sync.dma_start(
                    xbuf[k % 2].ap().rearrange("p (c n) -> p c n", n=T),
                    xT3[:, :, k * T:(k + 1) * T],
                ).then_inc(sm_xc if k % 2 == 1 else sm_xd, 16)

        @block.gpsimd
        def _(gpsimd):
            gpsimd.dma_start(
                wInB.ap().rearrange("p (c n) -> p c n", n=2048),
                W_in.ap().rearrange("(c p) n -> p c n", p=128),
            ).then_inc(sm_wi, 16)
            w1src = w1T.ap().rearrange("(c p) n -> p c n", p=128)
            w1dst = w1Q.ap().rearrange("p (c n) -> p c n", n=4096)
            gpsimd.dma_start(w1dst[:, :, 0:2048], w1src[:, :, 0:2048]).then_inc(sm_w1a, 16)
            gpsimd.dma_start(w1dst[:, :, 2048:4096], w1src[:, :, 2048:4096]).then_inc(sm_w1b, 16)
            gpsimd.dma_start(
                wOutB.ap().rearrange("p (c n) -> p c n", n=1024),
                W_out.ap().rearrange("(c p) n -> p c n", p=128),
            ).then_inc(sm_wo, 16)
            gpsimd.dma_start(
                w2Q.ap().rearrange("p (c n) -> p c n", n=1024),
                w2T.ap().rearrange("(c p) n -> p c n", p=128),
            ).then_inc(sm_w2, 16)
            for g in range(2):
                gpsimd.dma_start(
                    maskb[:, (g % 2) * T:(g % 2 + 1) * T],
                    maskT3[:, g, 0:T],
                ).then_inc(sm_m0 if g % 2 == 0 else sm_m1, 16)
            for k in range(NT):
                for dcol in range(DC):
                    g = 8 * k + dcol
                    gpsimd.wait_ge(sm_fin, g + 1)
                    gpsimd.dma_start(
                        yout.ap()[dcol * 128:(dcol + 1) * 128, k * T:(k + 1) * T],
                        outst[:, (dcol % 2) * T:(dcol % 2 + 1) * T],
                    ).then_inc(sm_st0 if g % 2 == 0 else sm_st1, 16)
                    if g + 2 < NT * DC:
                        kn, dn = divmod(g + 2, 8)
                        gpsimd.dma_start(
                            maskb[:, (dn % 2) * T:(dn % 2 + 1) * T],
                            maskT3[:, dn, kn * T:(kn + 1) * T],
                        ).then_inc(sm_m0 if (g + 2) % 2 == 0 else sm_m1, 16)
            gpsimd.wait_ge(sm_st0, 16 * NT * DC // 2)
            gpsimd.wait_ge(sm_st1, 16 * NT * DC // 2)

        @block.vector
        def _(vector):
            vector.memset(dec[:], 0.9)
            vector.memset(onesB[:], 1.0)
            vector.memset(epsC[:], 1e-5).then_inc(sm_bq, 1)

            def v_bias_prologue():
                vector.wait_ge(sm_b, 64)
                vector.tensor_copy(b1q8[:], b1C[:])
                vector.tensor_copy(b2q8[:], b2C[:])
                vector.tensor_copy(b1qF[:], b1q8[:])
                vector.tensor_copy(b2qF[:], b2q8[:]).then_inc(sm_bq, 1)

            def v_cast(kk, dc):
                xk = xbuf[kk % 2]
                if dc == 0 and kk >= 1:
                    vector.wait_ge(sm_P, 8 * (kk - 1) + 8)
                if kk == 0:
                    if dc == 0:
                        vector.wait_ge(sm_xa, 16)
                    elif dc == 2:
                        vector.wait_ge(sm_xb, 16)
                    elif dc == 4:
                        vector.wait_ge(sm_xe, 16)
                    elif dc == 6:
                        vector.wait_ge(sm_xf, 16)
                elif dc == 0:
                    if kk % 2 == 1:
                        vector.wait_ge(sm_xc, 16 * ((kk + 1) // 2))
                    else:
                        vector.wait_ge(sm_xd, 16 * (kk // 2))
                c = dc * T
                vector.tensor_copy(xnB[:, c:c + T], xk[:, c:c + T])
                vector.tensor_tensor(sqB[:, c:c + T], xk[:, c:c + T], xk[:, c:c + T],
                                     ALU.mult).then_inc(sm_xbf, 1)

            def v_v0(kk):
                vector.wait_ge(sm_smm, 16 * kk + 16)
                if kk >= 2:
                    vector.wait_ge(sm_st0, 16 * (4 * (kk - 2) + 4))
                    vector.wait_ge(sm_st1, 16 * (4 * (kk - 2) + 4))
                sc = outst[:, T:2 * T]
                vector.tensor_copy(sc, ps_pg[:])
                vector.tensor_tensor(sc, sc, sc, ALU.mult)
                vector.scalar_tensor_tensor(outst[:, 0:T], ps_ph[:], 1024.0, sc,
                                            ALU.mult, ALU.subtract).then_inc(sm_v0, 1)

            def v_xn_dc(kk, dc):
                xk = xbuf[kk % 2]
                if dc == 0:
                    vector.wait_ge(sm_ln, kk + 1)
                    if kk >= 1:
                        vector.wait_ge(sm_W1, 32 * (kk - 1) + 32)
                c = dc * T
                inv = outst[:, 0:T]
                vector.scalar_tensor_tensor(tb[:], ps_pg[:], -1.0 / 1024.0,
                                            xk[:, c:c + T], ALU.mult, ALU.add)
                vector.tensor_tensor(xnB[:, c:c + T], tb[:], inv, ALU.mult)
                vector.tensor_tensor(xqQ[:, c:c + T], tb[:], inv,
                                     ALU.mult).then_inc(sm_xn, 1)

            def v_t1(k, ec):
                xk = xbuf[k % 2]
                abanks_v = [ps_a[0], ps_a[1], ps_h1[0], ps_h1[1]]
                vector.wait_ge(sm_A, 8 * k + ec + 1)
                c = ec * T
                vector.scalar_tensor_tensor(xk[:, c:c + T], abanks_v[ec % 4][:],
                                            boutC[:, ec:ec + 1], xk[:, c:c + T],
                                            ALU.add, ALU.add).then_inc(sm_t1, 1)

            def v_fin(k, dcol):
                xk = xbuf[k % 2]
                g = 8 * k + dcol
                vector.wait_ge(sm_y16, g + 1)
                vector.wait_ge(sm_m0 if g % 2 == 0 else sm_m1, 16 * (g // 2 + 1))
                if g - 2 >= 0:
                    vector.wait_ge(sm_st0 if g % 2 == 0 else sm_st1,
                                   16 * ((g - 2) // 2 + 1))
                r = (dcol % 2) * T
                c = dcol * T
                vector.tensor_tensor(outst[:, r:r + T], y16[:, r:r + T],
                                     maskb[:, r:r + T], ALU.mult)
                vector.tensor_tensor(outst[:, r:r + T], outst[:, r:r + T],
                                     xk[:, c:c + T], ALU.add).then_inc(sm_fin, 1)

            for dc in range(DC):
                v_cast(0, dc)
            v_v0(0)
            v_bias_prologue()
            for dc in range(DC):
                v_xn_dc(0, dc)
            for k in range(NT):
                for oc in range(DC):
                    vector.wait_ge(sm_gev, 8 * k + oc + 1)
                    r = (oc % 2) * T
                    c = oc * T
                    vector.scalar_tensor_tensor(tb[:], sg[:, r:r + T], 0.1,
                                                sh[:, r:r + T], ALU.mult, ALU.mult)
                    vector.tensor_tensor(tb[:], tb[:], phbB[:, r:r + T], ALU.mult)
                    init = 0.0 if k == 0 else hB[:, c + T - 1:c + T]
                    vector.tensor_tensor_scan(hB[:, c:c + T], dec[:], tb[:],
                                              init, ALU.mult, ALU.add).then_inc(sm_u, 1)
                    if k >= 1 and oc in (0, 1):
                        v_fin(k - 1, 6 + oc)
                for hc in range(HC):
                    vector.wait_ge(sm_h16, 32 * k + hc + 1)
                    vector.tensor_copy(hqQ[:, hc * T:(hc + 1) * T],
                                       h16[:, (hc % 2) * T:(hc % 2 + 1) * T]).then_inc(sm_hq, 1)
                    if k + 1 < NT and hc < 16 and hc % 2 == 0:
                        v_cast(k + 1, hc // 2)
                    if k + 1 < NT and hc == 26:
                        v_v0(k + 1)
                if k + 1 < NT:
                    for ec in range(DC):
                        v_t1(k, ec)
                        v_xn_dc(k + 1, ec)
                else:
                    for ec in range(DC):
                        v_t1(k, ec)
                for dcol in range(DC if k == NT - 1 else 6):
                    v_fin(k, dcol)

        @block.scalar
        def _(scalar):
            x03s = xbuf[0].ap().rearrange("p (c n) -> p c n", n=T)
            scalar.dma_start(x03s[:, 2:4, :], xT3[:, 2:4, 0:T]).then_inc(sm_xb, 16)
            scalar.dma_start(x03s[:, 6:8, :], xT3[:, 6:8, 0:T]).then_inc(sm_xf, 16)
            def s_ln(kk):
                scalar.wait_ge(sm_v0, kk + 1)
                scalar.activation(outst[:, T:2 * T], outst[:, 0:T], AF.Ln,
                                  bias=epsC[:], scale=float(2.0 ** -20))
                scalar.activation(outst[:, 0:T], outst[:, T:2 * T], AF.Exp,
                                  scale=-0.5).then_inc(sm_ln, 1)

            scalar.wait_ge(sm_bq, 1)
            s_ln(0)
            for k in range(NT):
                for oc in range(DC):
                    scalar.wait_ge(sm_P, 8 * k + oc + 1)
                    if k == 0 and oc == 0:
                        scalar.wait_ge(sm_bq, 2)
                    if 8 * k + oc - 2 >= 0:
                        scalar.wait_ge(sm_u, 8 * k + oc - 1)
                    r = (oc % 2) * T
                    pg_, ph_ = (ps_pg, ps_ph) if oc % 2 == 0 else (ps_a[0], ps_a[1])
                    scalar.activation(sg[:, r:r + T], pg_[:], AF.Sigmoid,
                                      bias=binC[:, oc:oc + 1], scale=1.0)
                    scalar.activation(sh[:, r:r + T], ph_[:], AF.Sigmoid,
                                      bias=binC[:, oc + 8:oc + 9], scale=1.0)
                    scalar.activation(phbB[:, r:r + T], ph_[:], AF.Identity,
                                      bias=binC[:, oc + 8:oc + 9], scale=1.0).then_inc(sm_gev, 1)
                hbank = [ps_h1[0], ps_h1[1], ps_y[0], ps_y[1]]
                for hc in range(HC):
                    if hc == 30 and k + 1 < NT:
                        s_ln(k + 1)
                    scalar.wait_ge(sm_W1, 32 * k + hc + 1)
                    if 32 * k + hc - 2 >= 0:
                        scalar.wait_ge(sm_hq, 32 * k + hc - 1)
                    r = (hc % 2) * T
                    scalar.activation(h16[:, r:r + T], hbank[hc % 4][:], AF.Relu,
                                      bias=b1qF[:, hc:hc + 1], scale=1.0).then_inc(sm_h16, 1)
                for dcol in range(DC):
                    scalar.wait_ge(sm_W2, 8 * k + dcol + 1)
                    if 8 * k + dcol - 2 >= 0:
                        scalar.wait_ge(sm_fin, 8 * k + dcol - 1)
                    r = (dcol % 2) * T
                    scalar.activation(y16[:, r:r + T], ps_y[dcol % 2][:], AF.Identity,
                                      bias=b2qF[:, dcol:dcol + 1], scale=1.0).then_inc(sm_y16, 1)

        @block.tensor
        def _(tensor):
            def t_stats_dc(kk, dc):
                if kk >= 1 and dc == 0:
                    tensor.wait_ge(sm_gev, 8 * (kk - 1) + 8)
                tensor.wait_ge(sm_xbf, 8 * kk + dc + 1)
                c = dc * T
                tensor.matmul(ps_pg[:], onesB[:], xnB[:, c:c + T],
                              start=(dc == 0), stop=(dc == DC - 1)).then_inc(sm_smm, 1)
                tensor.matmul(ps_ph[:], onesB[:], sqB[:, c:c + T],
                              start=(dc == 0), stop=(dc == DC - 1)).then_inc(sm_smm, 1)

            for dc in range(DC):
                t_stats_dc(0, dc)
            for k in range(NT):
                if k == 0:
                    tensor.wait_ge(sm_wi, 16)
                tensor.wait_ge(sm_xn, 8 * k + 8)
                for oc in range(DC):
                    if oc >= 2:
                        tensor.wait_ge(sm_gev, 8 * k + oc - 1)
                    if oc == 1 and k >= 1:
                        tensor.wait_ge(sm_t1, 8 * (k - 1) + 6)
                    pg_, ph_ = (ps_pg, ps_ph) if oc % 2 == 0 else (ps_a[0], ps_a[1])
                    for dc in range(DC):
                        st_, sp = (dc == 0), (dc == DC - 1)
                        base = dc * 2048
                        tensor.matmul(pg_[:], wInB[:, base + oc * 128:base + oc * 128 + 128],
                                      xnB[:, dc * T:(dc + 1) * T], start=st_, stop=sp)
                        mm = tensor.matmul(ph_[:], wInB[:, base + (oc + 8) * 128:base + (oc + 8) * 128 + 128],
                                           xnB[:, dc * T:(dc + 1) * T], start=st_, stop=sp)
                        if sp:
                            mm.then_inc(sm_P, 1)
                w1r = w1Q.ap().rearrange("p (c n) -> p c n", n=4096)
                xqr = xqQ.ap().rearrange("p (c n) -> p c n", n=T)
                hbank = [ps_h1[0], ps_h1[1], ps_y[0], ps_y[1]]
                for hc in range(HC):
                    if k == 0 and hc == 0:
                        tensor.wait_ge(sm_w1a, 16)
                    if k == 0 and hc == 16:
                        tensor.wait_ge(sm_w1b, 16)
                    if k + 1 < NT and 17 <= hc < 25:
                        t_stats_dc(k + 1, hc - 17)
                    if 32 * k + hc - 4 >= 0:
                        tensor.wait_ge(sm_h16, 32 * k + hc - 3)
                    if k >= 1 and hc < 2:
                        tensor.wait_ge(sm_t1, 8 * (k - 1) + 7 + hc)
                    if k >= 1 and 2 <= hc < 4:
                        tensor.wait_ge(sm_y16, 8 * (k - 1) + 7 + (hc - 2))
                    for dcp in range(DC // 2):
                        mm = tensor.matmul(hbank[hc % 4][:],
                                           w1r[:, 2 * dcp:2 * dcp + 2, hc * 128:hc * 128 + 128],
                                           xqr[:, 2 * dcp:2 * dcp + 2, :],
                                           start=(dcp == 0), stop=(dcp == DC // 2 - 1),
                                           perf_mode=mybir.MatmulPerfMode.DoubleRow)
                        if dcp == DC // 2 - 1:
                            mm.then_inc(sm_W1, 1)
                if k == 0:
                    tensor.wait_ge(sm_wo, 16)
                tensor.wait_ge(sm_u, 8 * k + 8)
                abanks = [ps_a[0], ps_a[1], ps_h1[0], ps_h1[1]]
                for ec in range(DC):
                    if 8 * k + ec - 4 >= 0:
                        tensor.wait_ge(sm_t1, 8 * k + ec - 3)
                    if ec % 4 == 2:
                        tensor.wait_ge(sm_h16, 32 * k + 29)
                    elif ec % 4 == 3:
                        tensor.wait_ge(sm_h16, 32 * k + 30)
                    for dc in range(DC):
                        base = dc * 1024
                        mm = tensor.matmul(abanks[ec % 4][:],
                                           wOutB[:, base + ec * 128:base + ec * 128 + 128],
                                           hB[:, dc * T:(dc + 1) * T],
                                           start=(dc == 0), stop=(dc == DC - 1))
                        if dc == DC - 1:
                            mm.then_inc(sm_A, 1)
                if k == 0:
                    tensor.wait_ge(sm_w2, 16)
                w2r = w2Q.ap().rearrange("p (c n) -> p c n", n=1024)
                hqr = hqQ.ap().rearrange("p (c n) -> p c n", n=T)
                for dcol in range(DC):
                    if dcol < 2:
                        tensor.wait_ge(sm_h16, 32 * k + 31 + dcol)
                    if 8 * k + dcol - 2 >= 0:
                        tensor.wait_ge(sm_y16, 8 * k + dcol - 1)
                    for hcp in range(HC // 2):
                        if dcol == 0:
                            tensor.wait_ge(sm_hq, 32 * k + 2 * hcp + 2)
                        mm = tensor.matmul(ps_y[dcol % 2][:],
                                           w2r[:, 2 * hcp:2 * hcp + 2, dcol * 128:dcol * 128 + 128],
                                           hqr[:, 2 * hcp:2 * hcp + 2, :],
                                           start=(hcp == 0), stop=(hcp == HC // 2 - 1),
                                           perf_mode=mybir.MatmulPerfMode.DoubleRow)
                        if hcp == HC // 2 - 1:
                            mm.then_inc(sm_W2, 1)
    return nc


_CACHE = {}


def kernel(L0_attn_x, L0_attn_W_in, L0_attn_b_in, L0_attn_W_out, L0_attn_b_out,
           L0_ffn_w1, L0_ffn_b1, L0_ffn_w2, L0_ffn_b2):
    x = np.asarray(L0_attn_x, dtype=np.float32)
    W_in = np.ascontiguousarray(np.asarray(L0_attn_W_in, dtype=np.float32))
    b_in = np.asarray(L0_attn_b_in, dtype=np.float32)
    W_out = np.ascontiguousarray(np.asarray(L0_attn_W_out, dtype=np.float32))
    b_out = np.asarray(L0_attn_b_out, dtype=np.float32)
    w1 = np.asarray(L0_ffn_w1, dtype=np.float32)
    b1 = np.asarray(L0_ffn_b1, dtype=np.float32)
    w2 = np.asarray(L0_ffn_w2, dtype=np.float32)
    b2 = np.asarray(L0_ffn_b2, dtype=np.float32)

    mask = _dropout_scale_mask()
    w1T = np.ascontiguousarray(w1.T)
    w2T = np.ascontiguousarray(w2.T)
    b_in_c = np.ascontiguousarray(b_in.reshape(16, 128).T)
    b_out_c = np.ascontiguousarray(b_out.reshape(8, 128).T)
    b1_c = np.ascontiguousarray(b1.reshape(32, 128).T)
    b2_c = np.ascontiguousarray(b2.reshape(8, 128).T)

    if "nc" not in _CACHE:
        _CACHE["nc"] = _build_nc()
    nc = _CACHE["nc"]

    in_maps = []
    for b in range(NCORES):
        in_maps.append({
            "xT": np.ascontiguousarray(x[b].T),
            "maskT": np.ascontiguousarray(mask[b].T),
            "W_in": W_in,
            "W_out": W_out,
            "w1T": w1T,
            "w2T": w2T,
            "b_in_c": b_in_c,
            "b_out_c": b_out_c,
            "b1_c": b1_c,
            "b2_c": b2_c,
        })
    res = run_bass_kernel_spmd(nc, in_maps, core_ids=list(range(NCORES)))
    out = np.empty((NCORES, S, D), dtype=np.float32)
    for b in range(NCORES):
        out[b] = np.asarray(res.results[b]["yout"]).T
    return out


# version 21
# speedup vs baseline: 1.0240x; 1.0240x over previous
import numpy as np
import ml_dtypes
import concourse.bass as bass
import concourse.mybir as mybir
from concourse.bass_utils import run_bass_kernel_spmd

F32 = mybir.dt.float32
F16 = mybir.dt.float16
BF16 = mybir.dt.bfloat16
FP8 = mybir.dt.float8e4
AF = mybir.ActivationFunctionType
ALU = mybir.AluOpType

D = 1024
S = 4096
T = 512
NT = S // T
DC = D // 128
HC = 4096 // 128
NCORES = 8


def _threefry2x32(k0, k1, x0, x1):
    rotations = [[13, 15, 26, 6], [17, 29, 16, 24]]
    k0 = np.uint32(k0)
    k1 = np.uint32(k1)
    ks2 = np.uint32(np.uint32(0x1BD11BDA) ^ k0 ^ k1)
    x0 = (x0 + k0).astype(np.uint32)
    x1 = (x1 + k1).astype(np.uint32)

    def rotl(x, r):
        return ((x << np.uint32(r)) | (x >> np.uint32(32 - r))).astype(np.uint32)

    ks = [k0, k1, ks2]
    for i in range(5):
        for r in rotations[i % 2]:
            x0 = (x0 + x1).astype(np.uint32)
            x1 = rotl(x1, r)
            x1 = (x1 ^ x0).astype(np.uint32)
        x0 = (x0 + ks[(i + 1) % 3]).astype(np.uint32)
        x1 = (x1 + ks[(i + 2) % 3] + np.uint32(i + 1)).astype(np.uint32)
    return x0, x1


def _dropout_scale_mask():
    n = NCORES * S * D
    idx = np.arange(n, dtype=np.uint64)
    c1 = (idx >> np.uint64(32)).astype(np.uint32)
    c2 = (idx & np.uint64(0xFFFFFFFF)).astype(np.uint32)
    r0, r1 = _threefry2x32(0, 42, c1, c2)
    bits = (r0 ^ r1).astype(np.uint32)
    fl = ((bits >> np.uint32(9)) | np.uint32(0x3F800000)).view(np.float32) - np.float32(1.0)
    keep = np.maximum(np.float32(0.0), fl) < np.float32(0.8)
    m = np.where(keep, np.float32(1.25), np.float32(0.0)).astype(ml_dtypes.float8_e4m3)
    return m.reshape(NCORES, S, D)


def _build_nc():
    nc = bass.Bass("TRN2", target_bir_lowering=False, detect_race_conditions=False)

    xT = nc.dram_tensor("xT", [D, S], F32, kind="ExternalInput")
    maskT = nc.dram_tensor("maskT", [D, S], FP8, kind="ExternalInput")
    W_in = nc.dram_tensor("W_in", [D, 2048], F32, kind="ExternalInput")
    W_out = nc.dram_tensor("W_out", [D, D], F32, kind="ExternalInput")
    w1T = nc.dram_tensor("w1T", [D, 4096], F32, kind="ExternalInput")
    w2T = nc.dram_tensor("w2T", [4096, D], F32, kind="ExternalInput")
    b_in_c = nc.dram_tensor("b_in_c", [128, 16], F32, kind="ExternalInput")
    b_out_c = nc.dram_tensor("b_out_c", [128, 8], F32, kind="ExternalInput")
    b1_c = nc.dram_tensor("b1_c", [128, 32], F32, kind="ExternalInput")
    b2_c = nc.dram_tensor("b2_c", [128, 8], F32, kind="ExternalInput")
    yout = nc.dram_tensor("yout", [D, S], F32, kind="ExternalOutput")

    from contextlib import ExitStack
    es = ExitStack()
    sb = lambda nm, shp, dt: es.enter_context(nc.sbuf_tensor(nm, shp, dt))
    sem = lambda nm: es.enter_context(nc.semaphore(nm))
    with es:
        wInB = sb("wInB", [128, DC * 2048], BF16)
        wOutB = sb("wOutB", [128, DC * 1024], BF16)
        w1Q = sb("w1Q", [128, DC * 4096], FP8)
        w2Q = sb("w2Q", [128, HC * 1024], FP8)
        xbuf = [sb("xbuf0", [128, DC * T], F32), sb("xbuf1", [128, DC * T], F32)]
        maskb = sb("maskb", [128, 2 * T], FP8)
        xnB = sb("xnB", [128, DC * T], BF16)
        xqQ = sb("xqQ", [128, DC * T], FP8)
        hB = sb("hB", [128, DC * T], BF16)
        hqQ = sb("hqQ", [128, HC * T], FP8)
        y16 = sb("y16", [128, 2 * T], F16)
        tb = sb("tb", [128, T], F32)
        sqB = sb("sqB", [128, DC * T], F16)
        phbB = sb("phbB", [128, 2 * T], BF16)
        sg = sb("sg", [128, 2 * T], BF16)
        sh = sb("sh", [128, 2 * T], BF16)
        h16 = sb("h16", [128, 2 * T], F16)
        outst = sb("outst", [128, 2 * T], F32)
        dec = sb("dec", [128, T], F32)
        epsC = sb("epsC", [128, 1], F32)
        onesB = sb("onesB", [128, 128], BF16)
        binC = sb("binC", [128, 16], F32)
        boutC = sb("boutC", [128, 8], F32)
        b1C = sb("b1C", [128, 32], F32)
        b1q8 = sb("b1q8", [128, 32], FP8)
        b1qF = sb("b1qF", [128, 32], F32)
        b2C = sb("b2C", [128, 8], F32)
        b2q8 = sb("b2q8", [128, 8], FP8)
        b2qF = sb("b2qF", [128, 8], F32)
        ps_y = [es.enter_context(nc.psum_tensor(f"ps_y{i}", [128, T], F32)) for i in range(2)]
        ps_pg = es.enter_context(nc.psum_tensor("ps_pg", [128, T], F32))
        ps_ph = es.enter_context(nc.psum_tensor("ps_ph", [128, T], F32))
        ps_h1 = [es.enter_context(nc.psum_tensor(f"ps_h1{i}", [128, T], F32)) for i in range(2)]
        ps_a = [es.enter_context(nc.psum_tensor(f"ps_a{i}", [128, T], F32)) for i in range(2)]

        sm_b = sem("sm_b")
        sm_xa = sem("sm_xa")
        sm_xb = sem("sm_xb")
        sm_xe = sem("sm_xe")
        sm_xf = sem("sm_xf")
        sm_xc = sem("sm_xc")
        sm_xd = sem("sm_xd")
        sm_m0 = sem("sm_m0")
        sm_m1 = sem("sm_m1")
        sm_wi = sem("sm_wi")
        sm_wj = sem("sm_wj")
        sm_w1a = sem("sm_w1a")
        sm_w1b = sem("sm_w1b")
        sm_wo = sem("sm_wo")
        sm_w2 = sem("sm_w2")
        sm_st0 = sem("sm_st0")
        sm_st1 = sem("sm_st1")
        sm_bq = sem("sm_bq")
        sm_xbf = sem("sm_xbf")
        sm_smm = sem("sm_smm")
        sm_v0 = sem("sm_v0")
        sm_ln = sem("sm_ln")
        sm_xn = sem("sm_xn")
        sm_P = sem("sm_P")
        sm_gev = sem("sm_gev")
        sm_u = sem("sm_u")
        sm_A = sem("sm_A")
        sm_t1 = sem("sm_t1")
        sm_W1 = sem("sm_W1")
        sm_h16 = sem("sm_h16")
        sm_hq = sem("sm_hq")
        sm_W2 = sem("sm_W2")
        sm_y16 = sem("sm_y16")
        sm_fin = sem("sm_fin")

        block = es.enter_context(nc.Block())

        xT3 = xT.ap().rearrange("(c p) n -> p c n", p=128)
        maskT3 = maskT.ap().rearrange("(c p) n -> p c n", p=128)

        @block.sync
        def _(sync):
            x03 = xbuf[0].ap().rearrange("p (c n) -> p c n", n=T)
            sync.dma_start(x03[:, 0:2, :], xT3[:, 0:2, 0:T]).then_inc(sm_xa, 16)
            sync.dma_start(x03[:, 4:6, :], xT3[:, 4:6, 0:T]).then_inc(sm_xe, 16)
            sync.dma_start(binC[:], b_in_c.ap()).then_inc(sm_b, 16)
            sync.dma_start(boutC[:], b_out_c.ap()).then_inc(sm_b, 16)
            sync.dma_start(b1C[:], b1_c.ap()).then_inc(sm_b, 16)
            sync.dma_start(b2C[:], b2_c.ap()).then_inc(sm_b, 16)
            for k in range(1, NT):
                if k >= 2:
                    sync.wait_ge(sm_fin, 8 * (k - 2) + 8)
                sync.dma_start(
                    xbuf[k % 2].ap().rearrange("p (c n) -> p c n", n=T),
                    xT3[:, :, k * T:(k + 1) * T],
                ).then_inc(sm_xc if k % 2 == 1 else sm_xd, 16)

        @block.gpsimd
        def _(gpsimd):
            wInd = wInB.ap().rearrange("p (c n) -> p c n", n=2048)
            wIns = W_in.ap().rearrange("(c p) n -> p c n", p=128)
            gpsimd.dma_start(wInd[:, 0:4, :], wIns[:, 0:4, :]).then_inc(sm_wi, 16)
            gpsimd.dma_start(wInd[:, 4:8, :], wIns[:, 4:8, :]).then_inc(sm_wj, 16)
            w1src = w1T.ap().rearrange("(c p) n -> p c n", p=128)
            w1dst = w1Q.ap().rearrange("p (c n) -> p c n", n=4096)
            gpsimd.dma_start(w1dst[:, :, 0:2048], w1src[:, :, 0:2048]).then_inc(sm_w1a, 16)
            gpsimd.dma_start(w1dst[:, :, 2048:4096], w1src[:, :, 2048:4096]).then_inc(sm_w1b, 16)
            gpsimd.dma_start(
                wOutB.ap().rearrange("p (c n) -> p c n", n=1024),
                W_out.ap().rearrange("(c p) n -> p c n", p=128),
            ).then_inc(sm_wo, 16)
            gpsimd.dma_start(
                w2Q.ap().rearrange("p (c n) -> p c n", n=1024),
                w2T.ap().rearrange("(c p) n -> p c n", p=128),
            ).then_inc(sm_w2, 16)
            for g in range(2):
                gpsimd.dma_start(
                    maskb[:, (g % 2) * T:(g % 2 + 1) * T],
                    maskT3[:, g, 0:T],
                ).then_inc(sm_m0 if g % 2 == 0 else sm_m1, 16)
            for k in range(NT):
                for dcol in range(DC):
                    g = 8 * k + dcol
                    gpsimd.wait_ge(sm_fin, g + 1)
                    gpsimd.dma_start(
                        yout.ap()[dcol * 128:(dcol + 1) * 128, k * T:(k + 1) * T],
                        outst[:, (dcol % 2) * T:(dcol % 2 + 1) * T],
                    ).then_inc(sm_st0 if g % 2 == 0 else sm_st1, 16)
                    if g + 2 < NT * DC:
                        kn, dn = divmod(g + 2, 8)
                        gpsimd.dma_start(
                            maskb[:, (dn % 2) * T:(dn % 2 + 1) * T],
                            maskT3[:, dn, kn * T:(kn + 1) * T],
                        ).then_inc(sm_m0 if (g + 2) % 2 == 0 else sm_m1, 16)
            gpsimd.wait_ge(sm_st0, 16 * NT * DC // 2)
            gpsimd.wait_ge(sm_st1, 16 * NT * DC // 2)

        @block.vector
        def _(vector):
            vector.memset(dec[:], 0.9)
            vector.memset(onesB[:], 1.0)
            vector.memset(epsC[:], 1e-5).then_inc(sm_bq, 1)

            def v_bias_prologue():
                vector.wait_ge(sm_b, 64)
                vector.tensor_copy(b1q8[:], b1C[:])
                vector.tensor_copy(b2q8[:], b2C[:])
                vector.tensor_copy(b1qF[:], b1q8[:])
                vector.tensor_copy(b2qF[:], b2q8[:]).then_inc(sm_bq, 1)

            def v_cast(kk, dc):
                xk = xbuf[kk % 2]
                if dc == 0 and kk >= 1:
                    vector.wait_ge(sm_P, 8 * (kk - 1) + 8)
                if kk == 0:
                    if dc == 0:
                        vector.wait_ge(sm_xa, 16)
                    elif dc == 2:
                        vector.wait_ge(sm_xb, 16)
                    elif dc == 4:
                        vector.wait_ge(sm_xe, 16)
                    elif dc == 6:
                        vector.wait_ge(sm_xf, 16)
                elif dc == 0:
                    if kk % 2 == 1:
                        vector.wait_ge(sm_xc, 16 * ((kk + 1) // 2))
                    else:
                        vector.wait_ge(sm_xd, 16 * (kk // 2))
                c = dc * T
                vector.tensor_copy(xnB[:, c:c + T], xk[:, c:c + T])
                vector.tensor_tensor(sqB[:, c:c + T], xk[:, c:c + T], xk[:, c:c + T],
                                     ALU.mult).then_inc(sm_xbf, 1)

            def v_v0(kk):
                vector.wait_ge(sm_smm, 16 * kk + 16)
                if kk >= 2:
                    vector.wait_ge(sm_st0, 16 * (4 * (kk - 2) + 4))
                    vector.wait_ge(sm_st1, 16 * (4 * (kk - 2) + 4))
                sc = outst[:, T:2 * T]
                vector.tensor_copy(sc, ps_pg[:])
                vector.tensor_tensor(sc, sc, sc, ALU.mult)
                vector.scalar_tensor_tensor(outst[:, 0:T], ps_ph[:], 1024.0, sc,
                                            ALU.mult, ALU.subtract).then_inc(sm_v0, 1)

            def v_xn_dc(kk, dc):
                xk = xbuf[kk % 2]
                if dc == 0:
                    vector.wait_ge(sm_ln, kk + 1)
                    if kk >= 1:
                        vector.wait_ge(sm_W1, 32 * (kk - 1) + 32)
                c = dc * T
                inv = outst[:, 0:T]
                vector.scalar_tensor_tensor(tb[:], ps_pg[:], -1.0 / 1024.0,
                                            xk[:, c:c + T], ALU.mult, ALU.add)
                vector.tensor_tensor(xnB[:, c:c + T], tb[:], inv, ALU.mult)
                vector.tensor_tensor(xqQ[:, c:c + T], tb[:], inv,
                                     ALU.mult).then_inc(sm_xn, 1)

            def v_t1(k, ec):
                xk = xbuf[k % 2]
                abanks_v = [ps_a[0], ps_a[1], ps_h1[0], ps_h1[1]]
                vector.wait_ge(sm_A, 8 * k + ec + 1)
                c = ec * T
                vector.scalar_tensor_tensor(xk[:, c:c + T], abanks_v[ec % 4][:],
                                            boutC[:, ec:ec + 1], xk[:, c:c + T],
                                            ALU.add, ALU.add).then_inc(sm_t1, 1)

            def v_fin(k, dcol):
                xk = xbuf[k % 2]
                g = 8 * k + dcol
                vector.wait_ge(sm_y16, g + 1)
                vector.wait_ge(sm_m0 if g % 2 == 0 else sm_m1, 16 * (g // 2 + 1))
                if g - 2 >= 0:
                    vector.wait_ge(sm_st0 if g % 2 == 0 else sm_st1,
                                   16 * ((g - 2) // 2 + 1))
                r = (dcol % 2) * T
                c = dcol * T
                vector.tensor_tensor(outst[:, r:r + T], y16[:, r:r + T],
                                     maskb[:, r:r + T], ALU.mult)
                vector.tensor_tensor(outst[:, r:r + T], outst[:, r:r + T],
                                     xk[:, c:c + T], ALU.add).then_inc(sm_fin, 1)

            for dc in range(DC):
                v_cast(0, dc)
            v_v0(0)
            v_bias_prologue()
            for dc in range(DC):
                v_xn_dc(0, dc)
            for k in range(NT):
                for oc in range(DC):
                    vector.wait_ge(sm_gev, 8 * k + oc + 1)
                    r = (oc % 2) * T
                    c = oc * T
                    vector.scalar_tensor_tensor(tb[:], sg[:, r:r + T], 0.1,
                                                sh[:, r:r + T], ALU.mult, ALU.mult)
                    vector.tensor_tensor(tb[:], tb[:], phbB[:, r:r + T], ALU.mult)
                    init = 0.0 if k == 0 else hB[:, c + T - 1:c + T]
                    vector.tensor_tensor_scan(hB[:, c:c + T], dec[:], tb[:],
                                              init, ALU.mult, ALU.add).then_inc(sm_u, 1)
                    if k >= 1 and oc in (0, 1):
                        v_fin(k - 1, 6 + oc)
                for hc in range(HC):
                    vector.wait_ge(sm_h16, 32 * k + hc + 1)
                    vector.tensor_copy(hqQ[:, hc * T:(hc + 1) * T],
                                       h16[:, (hc % 2) * T:(hc % 2 + 1) * T]).then_inc(sm_hq, 1)
                    if k + 1 < NT and hc < 16 and hc % 2 == 0:
                        v_cast(k + 1, hc // 2)
                    if k + 1 < NT and hc == 26:
                        v_v0(k + 1)
                if k + 1 < NT:
                    for ec in range(DC):
                        v_t1(k, ec)
                        v_xn_dc(k + 1, ec)
                else:
                    for ec in range(DC):
                        v_t1(k, ec)
                for dcol in range(DC if k == NT - 1 else 6):
                    v_fin(k, dcol)

        @block.scalar
        def _(scalar):
            x03s = xbuf[0].ap().rearrange("p (c n) -> p c n", n=T)
            scalar.dma_start(x03s[:, 2:4, :], xT3[:, 2:4, 0:T]).then_inc(sm_xb, 16)
            scalar.dma_start(x03s[:, 6:8, :], xT3[:, 6:8, 0:T]).then_inc(sm_xf, 16)
            def s_ln(kk):
                scalar.wait_ge(sm_v0, kk + 1)
                scalar.activation(outst[:, T:2 * T], outst[:, 0:T], AF.Ln,
                                  bias=epsC[:], scale=float(2.0 ** -20))
                scalar.activation(outst[:, 0:T], outst[:, T:2 * T], AF.Exp,
                                  scale=-0.5).then_inc(sm_ln, 1)

            scalar.wait_ge(sm_bq, 1)
            s_ln(0)
            for k in range(NT):
                for oc in range(DC):
                    scalar.wait_ge(sm_P, 8 * k + oc + 1)
                    if k == 0 and oc == 0:
                        scalar.wait_ge(sm_bq, 2)
                    if 8 * k + oc - 2 >= 0:
                        scalar.wait_ge(sm_u, 8 * k + oc - 1)
                    r = (oc % 2) * T
                    pg_, ph_ = (ps_a[0], ps_a[1]) if oc % 2 == 0 else (ps_pg, ps_ph)
                    scalar.activation(sg[:, r:r + T], pg_[:], AF.Sigmoid,
                                      bias=binC[:, oc:oc + 1], scale=1.0)
                    scalar.activation(sh[:, r:r + T], ph_[:], AF.Sigmoid,
                                      bias=binC[:, oc + 8:oc + 9], scale=1.0)
                    scalar.activation(phbB[:, r:r + T], ph_[:], AF.Identity,
                                      bias=binC[:, oc + 8:oc + 9], scale=1.0).then_inc(sm_gev, 1)
                hbank = [ps_h1[0], ps_h1[1], ps_y[0], ps_y[1]]
                for hc in range(HC):
                    if hc == 30 and k + 1 < NT:
                        s_ln(k + 1)
                    scalar.wait_ge(sm_W1, 32 * k + hc + 1)
                    if 32 * k + hc - 2 >= 0:
                        scalar.wait_ge(sm_hq, 32 * k + hc - 1)
                    r = (hc % 2) * T
                    scalar.activation(h16[:, r:r + T], hbank[hc % 4][:], AF.Relu,
                                      bias=b1qF[:, hc:hc + 1], scale=1.0).then_inc(sm_h16, 1)
                for dcol in range(DC):
                    scalar.wait_ge(sm_W2, 8 * k + dcol + 1)
                    if 8 * k + dcol - 2 >= 0:
                        scalar.wait_ge(sm_fin, 8 * k + dcol - 1)
                    r = (dcol % 2) * T
                    scalar.activation(y16[:, r:r + T], ps_y[dcol % 2][:], AF.Identity,
                                      bias=b2qF[:, dcol:dcol + 1], scale=1.0).then_inc(sm_y16, 1)

        @block.tensor
        def _(tensor):
            def t_stats_dc(kk, dc):
                if kk >= 1 and dc == 0:
                    tensor.wait_ge(sm_gev, 8 * (kk - 1) + 8)
                tensor.wait_ge(sm_xbf, 8 * kk + dc + 1)
                c = dc * T
                tensor.matmul(ps_pg[:], onesB[:], xnB[:, c:c + T],
                              start=(dc == 0), stop=(dc == DC - 1)).then_inc(sm_smm, 1)
                tensor.matmul(ps_ph[:], onesB[:], sqB[:, c:c + T],
                              start=(dc == 0), stop=(dc == DC - 1)).then_inc(sm_smm, 1)

            for dc in range(DC):
                t_stats_dc(0, dc)
            for k in range(NT):
                for oc in range(DC):
                    if oc == 0 and k >= 1:
                        tensor.wait_ge(sm_t1, 8 * (k - 1) + 6)
                    if oc == 1:
                        tensor.wait_ge(sm_xn, 8 * k + 8)
                    if oc >= 2:
                        tensor.wait_ge(sm_gev, 8 * k + oc - 1)
                    pg_, ph_ = (ps_a[0], ps_a[1]) if oc % 2 == 0 else (ps_pg, ps_ph)
                    for dc in range(DC):
                        if oc == 0:
                            tensor.wait_ge(sm_xn, 8 * k + dc + 1)
                            if k == 0 and dc == 0:
                                tensor.wait_ge(sm_wi, 16)
                            if k == 0 and dc == 4:
                                tensor.wait_ge(sm_wj, 16)
                        st_, sp = (dc == 0), (dc == DC - 1)
                        base = dc * 2048
                        tensor.matmul(pg_[:], wInB[:, base + oc * 128:base + oc * 128 + 128],
                                      xnB[:, dc * T:(dc + 1) * T], start=st_, stop=sp)
                        mm = tensor.matmul(ph_[:], wInB[:, base + (oc + 8) * 128:base + (oc + 8) * 128 + 128],
                                           xnB[:, dc * T:(dc + 1) * T], start=st_, stop=sp)
                        if sp:
                            mm.then_inc(sm_P, 1)
                w1r = w1Q.ap().rearrange("p (c n) -> p c n", n=4096)
                xqr = xqQ.ap().rearrange("p (c n) -> p c n", n=T)
                hbank = [ps_h1[0], ps_h1[1], ps_y[0], ps_y[1]]
                for hc in range(HC):
                    if k == 0 and hc == 0:
                        tensor.wait_ge(sm_w1a, 16)
                    if k == 0 and hc == 16:
                        tensor.wait_ge(sm_w1b, 16)
                    if k + 1 < NT and 17 <= hc < 25:
                        t_stats_dc(k + 1, hc - 17)
                    if 32 * k + hc - 4 >= 0:
                        tensor.wait_ge(sm_h16, 32 * k + hc - 3)
                    if k >= 1 and hc < 2:
                        tensor.wait_ge(sm_t1, 8 * (k - 1) + 7 + hc)
                    if k >= 1 and 2 <= hc < 4:
                        tensor.wait_ge(sm_y16, 8 * (k - 1) + 7 + (hc - 2))
                    for dcp in range(DC // 2):
                        mm = tensor.matmul(hbank[hc % 4][:],
                                           w1r[:, 2 * dcp:2 * dcp + 2, hc * 128:hc * 128 + 128],
                                           xqr[:, 2 * dcp:2 * dcp + 2, :],
                                           start=(dcp == 0), stop=(dcp == DC // 2 - 1),
                                           perf_mode=mybir.MatmulPerfMode.DoubleRow)
                        if dcp == DC // 2 - 1:
                            mm.then_inc(sm_W1, 1)
                if k == 0:
                    tensor.wait_ge(sm_wo, 16)
                tensor.wait_ge(sm_u, 8 * k + 8)
                abanks = [ps_a[0], ps_a[1], ps_h1[0], ps_h1[1]]
                for ec in range(DC):
                    if 8 * k + ec - 4 >= 0:
                        tensor.wait_ge(sm_t1, 8 * k + ec - 3)
                    if ec % 4 == 2:
                        tensor.wait_ge(sm_h16, 32 * k + 29)
                    elif ec % 4 == 3:
                        tensor.wait_ge(sm_h16, 32 * k + 30)
                    for dc in range(DC):
                        base = dc * 1024
                        mm = tensor.matmul(abanks[ec % 4][:],
                                           wOutB[:, base + ec * 128:base + ec * 128 + 128],
                                           hB[:, dc * T:(dc + 1) * T],
                                           start=(dc == 0), stop=(dc == DC - 1))
                        if dc == DC - 1:
                            mm.then_inc(sm_A, 1)
                if k == 0:
                    tensor.wait_ge(sm_w2, 16)
                w2r = w2Q.ap().rearrange("p (c n) -> p c n", n=1024)
                hqr = hqQ.ap().rearrange("p (c n) -> p c n", n=T)
                for dcol in range(DC):
                    if dcol < 2:
                        tensor.wait_ge(sm_h16, 32 * k + 31 + dcol)
                    if 8 * k + dcol - 2 >= 0:
                        tensor.wait_ge(sm_y16, 8 * k + dcol - 1)
                    for hcp in range(HC // 2):
                        if dcol == 0:
                            tensor.wait_ge(sm_hq, 32 * k + 2 * hcp + 2)
                        mm = tensor.matmul(ps_y[dcol % 2][:],
                                           w2r[:, 2 * hcp:2 * hcp + 2, dcol * 128:dcol * 128 + 128],
                                           hqr[:, 2 * hcp:2 * hcp + 2, :],
                                           start=(hcp == 0), stop=(hcp == HC // 2 - 1),
                                           perf_mode=mybir.MatmulPerfMode.DoubleRow)
                        if hcp == HC // 2 - 1:
                            mm.then_inc(sm_W2, 1)
    return nc


_CACHE = {}


def kernel(L0_attn_x, L0_attn_W_in, L0_attn_b_in, L0_attn_W_out, L0_attn_b_out,
           L0_ffn_w1, L0_ffn_b1, L0_ffn_w2, L0_ffn_b2):
    x = np.asarray(L0_attn_x, dtype=np.float32)
    W_in = np.ascontiguousarray(np.asarray(L0_attn_W_in, dtype=np.float32))
    b_in = np.asarray(L0_attn_b_in, dtype=np.float32)
    W_out = np.ascontiguousarray(np.asarray(L0_attn_W_out, dtype=np.float32))
    b_out = np.asarray(L0_attn_b_out, dtype=np.float32)
    w1 = np.asarray(L0_ffn_w1, dtype=np.float32)
    b1 = np.asarray(L0_ffn_b1, dtype=np.float32)
    w2 = np.asarray(L0_ffn_w2, dtype=np.float32)
    b2 = np.asarray(L0_ffn_b2, dtype=np.float32)

    mask = _dropout_scale_mask()
    w1T = np.ascontiguousarray(w1.T)
    w2T = np.ascontiguousarray(w2.T)
    b_in_c = np.ascontiguousarray(b_in.reshape(16, 128).T)
    b_out_c = np.ascontiguousarray(b_out.reshape(8, 128).T)
    b1_c = np.ascontiguousarray(b1.reshape(32, 128).T)
    b2_c = np.ascontiguousarray(b2.reshape(8, 128).T)

    if "nc" not in _CACHE:
        _CACHE["nc"] = _build_nc()
    nc = _CACHE["nc"]

    in_maps = []
    for b in range(NCORES):
        in_maps.append({
            "xT": np.ascontiguousarray(x[b].T),
            "maskT": np.ascontiguousarray(mask[b].T),
            "W_in": W_in,
            "W_out": W_out,
            "w1T": w1T,
            "w2T": w2T,
            "b_in_c": b_in_c,
            "b_out_c": b_out_c,
            "b1_c": b1_c,
            "b2_c": b2_c,
        })
    res = run_bass_kernel_spmd(nc, in_maps, core_ids=list(range(NCORES)))
    out = np.empty((NCORES, S, D), dtype=np.float32)
    for b in range(NCORES):
        out[b] = np.asarray(res.results[b]["yout"]).T
    return out
